# version 05
# baseline (speedup 1.0000x reference)
import contextlib
import numpy as np
import ml_dtypes

import concourse.bass as bass
import concourse.mybir as mybir
from concourse.bass_utils import run_bass_kernel_spmd

F32 = mybir.dt.float32
BF16 = mybir.dt.bfloat16
ALU = mybir.AluOpType
AF = mybir.ActivationFunctionType
AX = mybir.AxisListType

H = 1024
C = 2048
NCC = 16
DST = 16
DTR = 64
II = 2816
W = 128
MAIN = 2048
SLAB = 3 + W + MAIN
NBLK = 4
L = 512

ENGS = ("v", "s", "t", "g", "qs")
_QUEUES = ("qs",)


class T:
    __slots__ = ("w", "r", "name", "qsem")

    def __init__(self, name=""):
        self.w = None
        self.r = []
        self.name = name
        self.qsem = None


class Sched:

    def __init__(self):
        self.q = {e: [] for e in ENGS}
        self.cnt = {}
        self.opidx = {e: 0 for e in ENGS}
        self.mark = {e: {} for e in ENGS}
        self.nqsem = 0

    def op(self, e, body, reads=(), writes=(), gap=6):
        raw, war = {}, {}

        def add(d, dep):
            if dep is None:
                return
            de, dv, di = dep
            cur = d.get(de)
            if cur is None or dv > cur[0]:
                d[de] = (dv, di)

        for t in reads:
            add(raw, t.w)
        for t in writes:
            add(war, t.w)
            for r in t.r:
                add(war, r)

        if e == "qs":
            assert len(writes) == 1
            t0 = writes[0]
            if t0.qsem is None:
                t0.qsem = f"q{self.nqsem}"
                self.nqsem += 1
            sem = t0.qsem
            amt = 16
        else:
            sem = e
            amt = 1

        waits = []
        for kind, deps in (("raw", raw), ("war", war)):
            for de, (dv, di) in deps.items():
                if de == e and de not in _QUEUES:
                    if kind == "raw" and self.opidx[e] - di < gap:
                        waits.append((de, dv))
                    continue
                m = self.mark[e].get(de, 0)
                if dv > m:
                    waits.append((de, dv))
                    self.mark[e][de] = dv

        self.cnt[sem] = self.cnt.get(sem, 0) + amt
        tag = (sem, self.cnt[sem], self.opidx[e])
        self.opidx[e] += 1
        self.q[e].append((waits, body, sem, amt))
        for t in writes:
            t.w = tag
            t.r = []
        for t in reads:
            t.r.append(tag)
        return tag


def _slab_rows_for_core(x, k):
    seq, half = k // 2, k % 2
    xs = np.asarray(x[seq])
    if half == 0:
        pad = np.zeros((3 + W, xs.shape[1]), xs.dtype)
        return np.ascontiguousarray(np.concatenate([pad, xs[0:MAIN]], axis=0))
    return np.ascontiguousarray(xs[MAIN - (3 + W): 2 * MAIN])


def build_nc():
    nc = bass.Bass("TRN2", target_bir_lowering=False)

    d_x = nc.dram_tensor("x_slab", [SLAB, H], F32, kind="ExternalInput").ap()
    d_win = nc.dram_tensor("w_in", [H, 2 * C], F32, kind="ExternalInput").ap()
    d_cw = nc.dram_tensor("conv_w", [4, C], F32, kind="ExternalInput").ap()
    d_cb = nc.dram_tensor("conv_b", [1, C], F32, kind="ExternalInput").ap()
    d_wx = nc.dram_tensor("w_x", [C, 96], F32, kind="ExternalInput").ap()
    d_wdt = nc.dram_tensor("w_dt", [DTR, C], F32, kind="ExternalInput").ap()
    d_bdt = nc.dram_tensor("b_dt", [1, C], F32, kind="ExternalInput").ap()
    d_alog = nc.dram_tensor("a_log", [C, DST], F32, kind="ExternalInput").ap()
    d_dd = nc.dram_tensor("d_vec", [1, C], F32, kind="ExternalInput").ap()
    d_wout = nc.dram_tensor("w_out", [C, H], F32, kind="ExternalInput").ap()
    d_wg = nc.dram_tensor("w_gate", [II, H], F32, kind="ExternalInput").ap()
    d_wu = nc.dram_tensor("w_up", [II, H], F32, kind="ExternalInput").ap()
    d_wd = nc.dram_tensor("w_down", [H, II], F32, kind="ExternalInput").ap()
    d_mask = nc.dram_tensor("wmask", [1, W], BF16, kind="ExternalInput").ap()
    d_id = nc.dram_tensor("identw", [128, 128], BF16, kind="ExternalInput").ap()
    d_out = nc.dram_tensor("out", [MAIN, H], F32, kind="ExternalOutput").ap()

    s_win = nc.dram_tensor("scr_win", [32, 128, 8, 128], BF16, kind="Internal").ap()
    s_wout = nc.dram_tensor("scr_wout", [C, H], BF16, kind="Internal").ap()
    s_wgT = nc.dram_tensor("scr_wgT", [H, II], BF16, kind="Internal").ap()
    s_wuT = nc.dram_tensor("scr_wuT", [H, II], BF16, kind="Internal").ap()
    s_wdT = nc.dram_tensor("scr_wdT", [II, H], BF16, kind="Internal").ap()
    s_bc = nc.dram_tensor("scr_bc", [32, L], BF16, kind="Internal").ap()

    ctx = contextlib.ExitStack()
    sb = lambda name, shape, dt: ctx.enter_context(nc.sbuf_tensor(name, shape, dt))
    ps = lambda name, shape, dt: ctx.enter_context(nc.psum_tensor(name, shape, dt))

    identT = sb("identT", [128, 128], BF16)
    A_t = sb("A_t", [128, NCC * DST], F32)
    Dv = sb("Dv", [128, NCC], F32)
    cbv = sb("cbv", [128, NCC], F32)
    bdtv = sb("bdtv", [128, NCC], F32)
    cwv = sb("cwv", [128, NCC * 4], F32)
    wx_t = sb("wx_t", [128, NCC * 96], BF16)
    wdt_t = sb("wdt_t", [DTR, C], BF16)
    maskrep = sb("maskrep", [128, W], BF16)
    epsv = sb("epsv", [128, 1], F32)
    _trev = sb("trev", [128, 128], BF16)
    trev = [_trev, _trev]
    xre = [sb(f"xre{i}", [128, H], F32) for i in range(2)]
    _xnscr = sb("xnscr", [128, H], F32)
    xnscr = [_xnscr, _xnscr]
    _xnb = sb("xnb", [128, H], BF16)
    xnb = [_xnb, _xnb]
    _xqt = sb("xqt", [128, H], BF16)
    xqt = [_xqt, _xqt]
    xnT = sb("xnT", [128, 8 * L], BF16)
    xqT = sb("xqT", [128, 8 * L], BF16)
    lnsum = sb("lnsum", [128, 8], F32)
    lnm = sb("lnm", [128, 8], F32)
    lns2 = sb("lns2", [128, 8], F32)
    lnv = sb("lnv", [128, 8], F32)
    lnr = sb("lnr", [128, 8], F32)
    qbm = sb("qbm", [128, 32], F32)
    qsc = sb("qsc", [128, 32], F32)
    qrc = sb("qrc", [128, 32], F32)
    qa = sb("qa", [128, 512], F32)
    qsg = sb("qsg", [128, 512], BF16)
    qL = [sb(f"qL{i}", [128, 512], BF16) for i in range(3)]
    _rawc = sb("rawc", [128, 515], BF16)
    rawc = [_rawc, _rawc]
    rawtail = sb("rawtail", [128, NCC * 3], BF16)
    _cvb = sb("cvb", [128, L], BF16)
    cvb = [_cvb, _cvb]
    hT = sb("hT", [128, NCC * L], BF16)
    gats = sb("gats", [128, NCC * L], BF16)
    ssm_ts = sb("ssm_ts", [DTR, L], BF16)
    ssm_bc = sb("ssm_bc", [32, L], BF16)
    sps = sb("sps", [128, L], F32)
    qab = sps
    dttc = [sb(f"dttc{i}", [128, L], BF16) for i in range(2)]
    dthc = [sb(f"dthc{i}", [128, L], BF16) for i in range(2)]
    brep = sb("brep", [128, DST * L], BF16)
    crep = sb("crep", [128, DST * L], BF16)
    dab = [sb(f"dab{i}", [128, L], BF16) for i in range(2)]
    dbb = [sb(f"dbb{i}", [128, L], BF16) for i in range(2)]
    stt = sb("stt", [128, DST * L], BF16)
    carry = sb("carry", [128, NCC * DST], F32)
    _ycf = sb("ycf", [128, L], F32)
    ycf = [_ycf, _ycf]
    ydt = sb("ydt", [128, NCC * L], BF16)
    winb = [sb(f"winb{i}", [128, 8 * 128], BF16) for i in range(2)]
    woutb = [sb(f"woutb{i}", [128, 512], BF16) for i in range(2)]
    wdnb = [sb(f"wdnb{i}", [128, 512], BF16) for i in range(2)]
    _sgb = sb("sgb", [128, 512], BF16)
    sgb = [_sgb, _sgb]
    _hpc = sb("hpc", [128, 512], BF16)
    hpc = [_hpc, _hpc]
    _hqp = sb("hqp", [128, 512], BF16)
    hqp = [_hqp, _hqp]
    hqT = sb("hqT", [128, 2 * 22 * 128], BF16)
    outt = sb("outt", [128, 4 * H], F32)
    wprep = outt
    wprep_b_ap = ydt.ap().bitcast(BF16) if False else ydt

    pmm = [ps(f"pmm{i}", [128, 512], F32) for i in range(2)]
    pw = [ps(f"pw{i}", [128, 512], F32) for i in range(4)]
    psmA = ps("psmA", [64, 512], F32)
    psmB = ps("psmB", [32, 512], F32)
    ptr = [pw[2].ap().bitcast(BF16)[:, 0:128], pw[3].ap().bitcast(BF16)[:, 0:128]]

    sems = {e: ctx.enter_context(nc.semaphore(f"sem_{e}")) for e in ENGS}
    S = Sched()

    def getsem(name):
        if name not in sems:
            sems[name] = ctx.enter_context(nc.semaphore(name))
        return sems[name]

    t_ = {}

    def tile(name):
        if name not in t_:
            t_[name] = T(name)
        return t_[name]

    def tiles(name, n):
        return [tile(f"{name}{i}") for i in range(n)]

    tl_xre = tiles("xre", 2)
    tl_xnscr = [tile("xnscr"), tile("xnscr")]
    tl_xnb = [tile("xnb"), tile("xnb")]
    tl_xqt = [tile("xqt"), tile("xqt")]
    tl_trev = [tile("trev"), tile("trev")]
    tl_rawc = [tile("rawc"), tile("rawc")]
    tl_cvb = [tile("cvb"), tile("cvb")]
    t_sps = tile("sps")
    tl_dttc = tiles("dttc", 2)
    tl_dthc = tiles("dthc", 2)
    tl_dab = tiles("dab", 2)
    tl_dbb = tiles("dbb", 2)
    tl_ycf = [tile("ycf"), tile("ycf")]
    tl_winb = tiles("winb", 2)
    tl_woutb = tiles("woutb", 2)
    tl_wdnb = tiles("wdnb", 2)
    tl_sgb = [tile("sgb"), tile("sgb")]
    tl_hpc = [tile("hpc"), tile("hpc")]
    tl_hqp = [tile("hqp"), tile("hqp")]
    tl_pmm = tiles("pmm", 2)
    tl_pw = tiles("pw", 4)
    tl_qL = [tile("qL0"), tile("qL1"), tile("qL2")]
    t_wprep = tile("outt")
    t_wprep_b = tile("ydt")

    def dma(dst_ap, src_ap, reads, writes, slow=False):
        S.op("qs", lambda e: e.dma_start(out=dst_ap, in_=src_ap,
                                         allow_slow_non_contiguous=slow),
             reads=reads, writes=writes)

    def vop(body, reads, writes, gap=6):
        S.op("v", body, reads=reads, writes=writes, gap=gap)

    def sop(body, reads, writes, gap=6):
        S.op("s", body, reads=reads, writes=writes, gap=gap)

    def top(body, reads, writes):
        S.op("t", body, reads=reads, writes=writes, gap=0)

    def mm_group(psum_ap, pieces, ptile):
        n = len(pieces)
        for i, (lt, rt, rd) in enumerate(pieces):
            top(lambda e, lt=lt, rt=rt, i=i: e.matmul(
                psum_ap, lt, rt, start=(i == 0), stop=(i == n - 1),
                skip_group_check=True),
                reads=list(rd), writes=[ptile])

    GRID_W = [0.5, 0.5, 0.5, 0.5, 1.0, 1.0, 2.0]
    GRID_M = [0.25, 0.75, 1.25, 1.75, 2.5, 3.5, 5.0]

    t_bm, t_sc, t_rc = tile("qbm"), tile("qsc"), tile("qrc")
    t_qa, t_qab, t_qsg = tile("qa"), tile("sps"), tile("qsg")

    def quant(dst_ap, dst_tile, src_ap, src_tile, width):
        assert width <= 512 and width % 16 == 0
        nb = width // 16
        v3 = src_ap.rearrange("p (nb k) -> p nb k", k=16)
        bmv = qbm.ap()[:, 0:nb]
        scv = qsc.ap()[:, 0:nb]
        rcv = qrc.ap()[:, 0:nb]
        vop(lambda e: e.tensor_reduce(bmv, v3, AX.X, ALU.max,
                                      apply_absolute_value=True),
            reads=[src_tile], writes=[t_bm])
        vop(lambda e: e.tensor_scalar(scv, bmv, 1.0 / 6.0, 1e-12,
                                      ALU.mult, ALU.max),
            reads=[t_bm], writes=[t_sc])
        vop(lambda e: e.reciprocal(rcv, scv), reads=[t_sc], writes=[t_rc])
        rbc = rcv.unsqueeze(2).broadcast_to([128, nb, 16])
        av = qa.ap()[:, 0:width]
        a3 = av.rearrange("p (nb k) -> p nb k", k=16)
        vop(lambda e: e.tensor_tensor(a3, v3, rbc, ALU.mult),
            reads=[src_tile, t_rc], writes=[t_qa])
        abv = qab.ap()[:, 0:width]
        sop(lambda e: e.activation(abv, av, AF.Abs), reads=[t_qa], writes=[t_qab])
        sop(lambda e: e.activation(qsg.ap()[:, 0:width], av, AF.Sign),
            reads=[t_qa], writes=[t_qsg])
        slots = [0, 1, 2, 1, 2, 1, 2]
        sl_ap = [qL[s].ap()[:, 0:width] for s in slots]
        for i in range(7):
            vop(lambda e, i=i: e.tensor_scalar(
                sl_ap[i], abv, GRID_M[i], GRID_W[i], ALU.is_ge, ALU.mult),
                reads=[t_qab], writes=[tl_qL[slots[i]]], gap=1)
            if i >= 2:
                j = i - 1
                vop(lambda e, j=j: e.tensor_tensor(
                    sl_ap[0], sl_ap[0], sl_ap[j], ALU.add),
                    reads=[tl_qL[0], tl_qL[slots[j]]], writes=[tl_qL[0]], gap=2)
        vop(lambda e: e.tensor_tensor(sl_ap[0], sl_ap[0], sl_ap[6], ALU.add),
            reads=[tl_qL[0], tl_qL[2]], writes=[tl_qL[0]], gap=2)
        vop(lambda e: e.tensor_tensor(sl_ap[0], sl_ap[0], qsg.ap()[:, 0:width],
                                      ALU.mult),
            reads=[tl_qL[0], t_qsg], writes=[tl_qL[0]], gap=2)
        sbc = scv.unsqueeze(2).broadcast_to([128, nb, 16])
        d3 = dst_ap.rearrange("p (nb k) -> p nb k", k=16)
        a_3 = sl_ap[0].rearrange("p (nb k) -> p nb k", k=16)
        vop(lambda e: e.tensor_tensor(d3, a_3, sbc, ALU.mult),
            reads=[tl_qL[0], t_sc], writes=[dst_tile], gap=2)

    t_const = tile("consts")
    t_A = tile("A_t")
    t_mask = tile("maskrep")
    t_wx, t_wdt = tile("wx_t"), tile("wdt_t")

    dma(identT[:], d_id, reads=[], writes=[t_const])
    vop(lambda e: e.memset(epsv[:], 1e-5), reads=[], writes=[t_const])
    dma(wprep.ap()[:, 0:NCC * DST].rearrange("p (cc d) -> p cc d", d=DST),
        d_alog.rearrange("(cc p) d -> p cc d", p=128),
        reads=[t_wprep], writes=[t_wprep])
    sop(lambda e: e.activation(A_t[:], wprep.ap()[:, 0:NCC * DST], AF.Exp),
        reads=[t_wprep], writes=[t_A])
    vop(lambda e: e.tensor_scalar_mul(A_t[:], A_t[:], -1.0),
        reads=[t_A], writes=[t_A])
    dma(Dv[:], d_dd.rearrange("o (cc p) -> p (o cc)", p=128), reads=[], writes=[t_const], slow=True)
    dma(cbv[:], d_cb.rearrange("o (cc p) -> p (o cc)", p=128), reads=[], writes=[t_const], slow=True)
    dma(bdtv[:], d_bdt.rearrange("o (cc p) -> p (o cc)", p=128), reads=[], writes=[t_const], slow=True)
    for k in range(4):
        dma(cwv.ap().rearrange("p (cc k) -> p cc k", k=4)[:, :, k],
            d_cw[k:k + 1, :].rearrange("o (cc p) -> p (o cc)", p=128),
            reads=[], writes=[t_const], slow=True)
    dma(wprep.ap()[:, 0:NCC * 96].rearrange("p (cc m) -> p cc m", m=96),
        d_wx.rearrange("(cc p) m -> p cc m", p=128),
        reads=[t_wprep], writes=[t_wprep])
    vop(lambda e: e.tensor_copy(wx_t[:], wprep.ap()[:, 0:NCC * 96]),
        reads=[t_wprep], writes=[t_wx])
    dma(wprep.ap()[0:DTR, 0:C], d_wdt, reads=[t_wprep], writes=[t_wprep])
    vop(lambda e: e.tensor_copy(wdt_t[:], wprep.ap()[0:DTR, 0:C]),
        reads=[t_wprep], writes=[t_wdt])
    dma(maskrep[:], d_mask.partition_broadcast(128)[:, 0, :], reads=[], writes=[t_mask])

    t_swin = tile("scr_win")
    for kc in range(8):
        dma(wprep.ap(), d_win[kc * 128:(kc + 1) * 128, :],
            reads=[t_wprep], writes=[t_wprep])
        vop(lambda e: e.tensor_copy(ydt.ap()[:, 0:4096], wprep.ap()),
            reads=[t_wprep], writes=[t_wprep_b])
        for cb in range(32):
            dma(s_win[cb, :, kc, :], ydt.ap()[:, cb * 128:(cb + 1) * 128],
                reads=[t_wprep_b], writes=[t_swin])
    t_swout = tile("scr_wout")
    for i in range(16):
        dma(wprep.ap()[:, 0:H], d_wout[i * 128:(i + 1) * 128, :],
            reads=[t_wprep], writes=[t_wprep])
        vop(lambda e: e.tensor_copy(ydt.ap()[:, 0:H], wprep.ap()[:, 0:H]),
            reads=[t_wprep], writes=[t_wprep_b])
        dma(s_wout[i * 128:(i + 1) * 128, :], ydt.ap()[:, 0:H],
            reads=[t_wprep_b], writes=[t_swout])

    def prep_wq(src_dram, dst_dram, rows, cols, t_dst):
        ntile = rows // 128
        for i in range(ntile):
            dma(wprep.ap()[:, 0:cols], src_dram[i * 128:(i + 1) * 128, :],
                reads=[t_wprep], writes=[t_wprep])
            for p0 in range(0, cols, 512):
                pwid = min(512, cols - p0)
                quant(ydt.ap()[:, p0:p0 + pwid], t_wprep_b,
                      wprep.ap()[:, p0:p0 + pwid], t_wprep, pwid)
            for hcc in range(cols // 128):
                pb = hcc % 2
                top(lambda e, hcc=hcc, pb=pb: e.transpose(
                    ptr[pb], ydt.ap()[:, hcc * 128:(hcc + 1) * 128], identT[:]),
                    reads=[t_wprep_b, t_const], writes=[tl_pw[2 + pb]])
                sop(lambda e, pb=pb: e.copy(trev[pb][:], ptr[pb]),
                    reads=[tl_pw[2 + pb]], writes=[tl_trev[pb]])
                dma(dst_dram[hcc * 128:(hcc + 1) * 128, i * 128:(i + 1) * 128],
                    trev[pb][:], reads=[tl_trev[pb]], writes=[t_dst])

    t_swg, t_swu, t_swd = tile("scr_wgT"), tile("scr_wuT"), tile("scr_wdT")
    prep_wq(d_wg, s_wgT, II, H, t_swg)
    prep_wq(d_wu, s_wuT, II, H, t_swu)
    prep_wq(d_wd, s_wdT, H, II, t_swd)

    t_xnT, t_xqT = tile("xnT"), tile("xqT")
    t_ln = tile("ln_scr")
    t_rawtail = tile("rawtail")
    t_hT, t_gats = tile("hT"), tile("gats")
    t_ssmts, t_ssmbc, t_sbc = tile("ssm_ts"), tile("ssm_bc"), tile("scr_bc")
    t_dtT = tile("dtT")
    t_brep, t_crep = tile("brep"), tile("crep")
    t_stt, t_carry = tile("stt"), tile("carry")
    t_ydt = tile("ydt")
    t_psmA, t_psmB = tile("psmA"), tile("psmB")
    t_hqT = tile("hqT")
    t_outt = tile("outt")
    t_out = tile("out_dram")

    def ln_tile(dst_f32, dst_tile, src_ap, src_tile, rows, col):
        mc = lnm.ap()[0:rows, col:col + 1]
        sc_ = lnsum.ap()[0:rows, col:col + 1]
        s2c = lns2.ap()[0:rows, col:col + 1]
        vc = lnv.ap()[0:rows, col:col + 1]
        rc = lnr.ap()[0:rows, col:col + 1]
        vop(lambda e: e.reduce_sum(sc_, src_ap, AX.X), reads=[src_tile], writes=[t_ln])
        sop(lambda e: e.activation(ydt.ap()[0:rows, 0:H], src_ap, AF.Square,
                                   accum_out=s2c),
            reads=[src_tile], writes=[tile("ydt"), t_ln])
        vop(lambda e: e.tensor_scalar_mul(mc, sc_, 1.0 / H), reads=[t_ln], writes=[t_ln])
        vop(lambda e: e.tensor_tensor(vc, mc, mc, ALU.mult), reads=[t_ln], writes=[t_ln])
        vop(lambda e: e.tensor_scalar_mul(s2c, s2c, 1.0 / H), reads=[t_ln], writes=[t_ln])
        vop(lambda e: e.tensor_tensor(vc, s2c, vc, ALU.subtract), reads=[t_ln], writes=[t_ln])
        sop(lambda e: e.activation(rc, vc, AF.Sqrt, bias=epsv.ap()[0:rows, :]),
            reads=[t_const, t_ln], writes=[t_ln])
        vop(lambda e: e.reciprocal(rc, rc), reads=[t_ln], writes=[t_ln])
        vop(lambda e: e.tensor_scalar(dst_f32, src_ap, mc, rc, ALU.subtract, ALU.mult),
            reads=[src_tile, t_ln], writes=[dst_tile])

    def transpose_to(dst_ap, dst_tile, src_ap, src_tile, rows, pidx):
        iv = identT[:] if rows == 128 else identT[0:rows, 0:rows]
        ov = ptr[pidx] if rows == 128 else ptr[pidx][:, 0:rows]
        top(lambda e: e.transpose(ov, src_ap, iv),
            reads=[src_tile, t_const], writes=[tl_pw[2 + pidx]])
        sop(lambda e: e.copy(dst_ap, ov), reads=[tl_pw[2 + pidx]], writes=[dst_tile])

    IC_W = [512] * 5 + [256]
    IC_O = [0, 512, 1024, 1536, 2048, 2560]

    for b in range(NBLK + 1):
        if b == 0:
            i0, Lb, r0, n_tt, xw = 0, W, 0, 2, 131
        else:
            i0, Lb, r0, n_tt, xw = W + (b - 1) * L, L, 3 + W + (b - 1) * L, 4, L

        xv = xnT.ap()[:, 0:8 * xw]

        def xvs(hc, a, bnd):
            return xnT.ap()[:, hc * xw + a: hc * xw + bnd]

        for t in range(n_tt):
            bb = t % 2
            if b == 0:
                rows = 128 if t == 0 else 3
                rr = 0 if t == 0 else 128
            else:
                rows, rr = 128, t * 128
            dma(xre[bb].ap()[0:rows, :], d_x[r0 + rr: r0 + rr + rows, :],
                reads=[tl_xre[bb]], writes=[tl_xre[bb]])
            src = xre[bb].ap()[0:rows, :]
            xn_ap = xnscr[bb].ap()[0:rows, :]
            ln_tile(xn_ap, tl_xnscr[bb], src, tl_xre[bb], rows, t)
            vop(lambda e, bb=bb, rows=rows: e.tensor_copy(
                xnb[bb].ap()[0:rows, :], xnscr[bb].ap()[0:rows, :]),
                reads=[tl_xnscr[bb]], writes=[tl_xnb[bb]])
            for hc in range(8):
                transpose_to(xvs(hc, rr, rr + rows), t_xnT,
                             xnb[bb].ap()[0:rows, hc * 128:(hc + 1) * 128],
                             tl_xnb[bb], rows, hc % 2)
            if b > 0:
                for qh in range(2):
                    quant(xqt[bb].ap()[0:rows, qh * 512:(qh + 1) * 512], tl_xqt[bb],
                          xnscr[bb].ap()[0:rows, qh * 512:(qh + 1) * 512],
                          tl_xnscr[bb], 512)
                for hc in range(8):
                    transpose_to(xqT.ap()[:, hc * L + rr: hc * L + rr + rows],
                                 t_xqT, xqt[bb].ap()[0:rows, hc * 128:(hc + 1) * 128],
                                 tl_xqt[bb], rows, hc % 2)

        def win_load(cb):
            wb = cb % 2
            dma(winb[wb][:], s_win[cb].rearrange("p k c -> p (k c)"),
                reads=[tl_winb[wb]], writes=[tl_winb[wb]])

        ncb = 32 if b > 0 else 16
        win_load(0)
        win_load(1)
        for cb in range(ncb):
            wb = cb % 2
            pb = cb % 2
            pieces = []
            for kc in range(8):
                lt = winb[wb].ap()[:, kc * 128:(kc + 1) * 128]
                rt = xvs(kc, 0, xw)
                pieces.append((lt, rt, [tl_winb[wb], t_xnT]))
            mm_group(pmm[pb].ap()[:, 0:xw], pieces, tl_pmm[pb])
            if cb + 2 < ncb:
                win_load(cb + 2)
            if cb < 16:
                cc = cb
                rb = cc % 2
                rwidth = xw if b == 0 else 515
                if b == 0:
                    vop(lambda e, rb=rb, pb=pb: e.tensor_copy(
                        rawc[rb].ap()[:, 0:131], pmm[pb].ap()[:, 0:131]),
                        reads=[tl_pmm[pb]], writes=[tl_rawc[rb]])
                else:
                    vop(lambda e, rb=rb, cc=cc: e.tensor_copy(
                        rawc[rb].ap()[:, 0:3], rawtail.ap()[:, cc * 3:cc * 3 + 3]),
                        reads=[t_rawtail], writes=[tl_rawc[rb]])
                    vop(lambda e, rb=rb, pb=pb: e.tensor_copy(
                        rawc[rb].ap()[:, 3:515], pmm[pb].ap()[:, 0:512]),
                        reads=[tl_pmm[pb]], writes=[tl_rawc[rb]], gap=1)
                vop(lambda e, rb=rb, cc=cc, Lb=Lb: e.tensor_scalar_mul(
                    cvb[rb].ap()[:, 0:Lb], rawc[rb].ap()[:, 0:Lb],
                    cwv.ap()[:, cc * 4:cc * 4 + 1]),
                    reads=[tl_rawc[rb]], writes=[tl_cvb[rb]])
                for k in (1, 2, 3):
                    vop(lambda e, rb=rb, cc=cc, k=k, Lb=Lb: e.scalar_tensor_tensor(
                        cvb[rb].ap()[:, 0:Lb], rawc[rb].ap()[:, k:k + Lb],
                        cwv.ap()[:, cc * 4 + k:cc * 4 + k + 1],
                        cvb[rb].ap()[:, 0:Lb], ALU.mult, ALU.add),
                        reads=[tl_rawc[rb], tl_cvb[rb]], writes=[tl_cvb[rb]], gap=1)
                sop(lambda e, rb=rb, cc=cc, Lb=Lb: e.activation(
                    hT.ap()[:, cc * L:cc * L + Lb], cvb[rb].ap()[:, 0:Lb], AF.Silu,
                    bias=cbv.ap()[:, cc:cc + 1]),
                    reads=[tl_cvb[rb]], writes=[t_hT])
                vop(lambda e, rb=rb, cc=cc, rwidth=rwidth: e.tensor_copy(
                    rawtail.ap()[:, cc * 3:cc * 3 + 3],
                    rawc[rb].ap()[:, rwidth - 3:rwidth]),
                    reads=[tl_rawc[rb]], writes=[t_rawtail])
            else:
                cc = cb - 16
                sop(lambda e, cc=cc, pb=pb, Lb=Lb: e.activation(
                    gats.ap()[:, cc * L:cc * L + Lb], pmm[pb].ap()[:, 0:Lb], AF.Silu),
                    reads=[tl_pmm[pb]], writes=[t_gats])

        piecesA, piecesB = [], []
        for cc in range(NCC):
            rt = hT.ap()[:, cc * L:cc * L + Lb]
            piecesA.append((wx_t.ap()[:, cc * 96:cc * 96 + 64], rt, [t_wx, t_hT]))
            piecesB.append((wx_t.ap()[:, cc * 96 + 64:cc * 96 + 96], rt, [t_wx, t_hT]))
        mm_group(psmA.ap()[:, 0:Lb], piecesA, t_psmA)
        mm_group(psmB.ap()[:, 0:Lb], piecesB, t_psmB)
        vop(lambda e, Lb=Lb: e.tensor_copy(ssm_ts.ap()[:, 0:Lb], psmA.ap()[:, 0:Lb]),
            reads=[t_psmA], writes=[t_ssmts])
        vop(lambda e, Lb=Lb: e.tensor_copy(ssm_bc.ap()[:, 0:Lb], psmB.ap()[:, 0:Lb]),
            reads=[t_psmB], writes=[t_ssmbc])
        dma(s_bc[:, 0:Lb], ssm_bc.ap()[:, 0:Lb], reads=[t_ssmbc], writes=[t_sbc])

        def bc_load(d):
            sl = d
            dma(brep.ap()[:, sl * L:sl * L + Lb],
                s_bc.partition_broadcast(128)[:, d, 0:Lb],
                reads=[t_sbc], writes=[t_brep])
            if b > 0:
                dma(crep.ap()[:, sl * L:sl * L + Lb],
                    s_bc.partition_broadcast(128)[:, DST + d, 0:Lb],
                    reads=[t_sbc], writes=[t_crep])

        st3 = stt.ap().rearrange("p (d l) -> p d l", l=L)
        for cc in range(NCC):
            pb = cc % 2
            dc = cc % 2
            db_ = cc % 2
            yb = cc % 2
            mm_group(pmm[pb].ap()[:, 0:Lb],
                     [(wdt_t.ap()[:, cc * 128:(cc + 1) * 128], ssm_ts.ap()[:, 0:Lb],
                       [t_wdt, t_ssmts])], tl_pmm[pb])
            sop(lambda e, pb=pb, cc=cc, Lb=Lb: e.activation(
                sps.ap()[:, 0:Lb], pmm[pb].ap()[:, 0:Lb], AF.Exp,
                bias=bdtv.ap()[:, cc:cc + 1]),
                reads=[tl_pmm[pb]], writes=[t_sps])
            vop(lambda e, Lb=Lb: e.tensor_scalar_add(
                sps.ap()[:, 0:Lb], sps.ap()[:, 0:Lb], 1.0),
                reads=[t_sps], writes=[t_sps])
            if b == 0:
                sop(lambda e, Lb=Lb: e.activation(
                    sps.ap()[:, 0:Lb], sps.ap()[:, 0:Lb], AF.Ln),
                    reads=[t_sps], writes=[t_sps])
                vop(lambda e, dc=dc, Lb=Lb: e.tensor_tensor(
                    dttc[dc].ap()[:, 0:Lb], sps.ap()[:, 0:Lb],
                    maskrep[:], ALU.mult),
                    reads=[t_sps, t_mask], writes=[tl_dttc[dc]])
            else:
                sop(lambda e, dc=dc, Lb=Lb: e.activation(
                    dttc[dc].ap()[:, 0:Lb], sps.ap()[:, 0:Lb], AF.Ln),
                    reads=[t_sps], writes=[tl_dttc[dc]])
            vop(lambda e, db_=db_, dc=dc, cc=cc, Lb=Lb: e.tensor_tensor(
                dthc[db_].ap()[:, 0:Lb], dttc[dc].ap()[:, 0:Lb],
                hT.ap()[:, cc * L:cc * L + Lb], ALU.mult),
                reads=[tl_dttc[dc], t_hT], writes=[tl_dthc[db_]])
            if b > 0:
                vop(lambda e, yb=yb, cc=cc, Lb=Lb: e.tensor_scalar_mul(
                    ycf[yb].ap()[:, 0:Lb], hT.ap()[:, cc * L:cc * L + Lb],
                    Dv.ap()[:, cc:cc + 1]),
                    reads=[t_hT], writes=[tl_ycf[yb]])

            def emit_da_db(d):
                ab_ = d % 2
                if cc == 0:
                    bc_load(d)
                sop(lambda e, ab_=ab_, dc=dc, cc=cc, d=d, Lb=Lb: e.activation(
                    dab[ab_].ap()[:, 0:Lb], dttc[dc].ap()[:, 0:Lb], AF.Exp,
                    scale=A_t.ap()[:, cc * DST + d:cc * DST + d + 1]),
                    reads=[tl_dttc[dc], t_A], writes=[tl_dab[ab_]])
                vop(lambda e, ab_=ab_, db_=db_, d=d, Lb=Lb: e.tensor_tensor(
                    dbb[ab_].ap()[:, 0:Lb], dthc[db_].ap()[:, 0:Lb],
                    brep.ap()[:, d * L:d * L + Lb], ALU.mult),
                    reads=[tl_dthc[db_], t_brep], writes=[tl_dbb[ab_]])

            def emit_scan(d):
                ab_ = d % 2
                if b == 0:
                    init = 0.0
                else:
                    init = carry.ap()[:, cc * DST + d:cc * DST + d + 1]
                vop(lambda e, ab_=ab_, d=d, init=init, Lb=Lb: e.tensor_tensor_scan(
                    st3[:, d, 0:Lb], dab[ab_].ap()[:, 0:Lb],
                    dbb[ab_].ap()[:, 0:Lb], init, ALU.mult, ALU.add),
                    reads=[tl_dab[ab_], tl_dbb[ab_], t_carry], writes=[t_stt],
                    gap=2)

            emit_da_db(0)
            for d in range(1, DST):
                emit_da_db(d)
                emit_scan(d - 1)
            emit_scan(DST - 1)
            vop(lambda e, cc=cc, Lb=Lb: e.tensor_copy(
                carry.ap()[:, cc * DST:(cc + 1) * DST], st3[:, :, Lb - 1]),
                reads=[t_stt], writes=[t_carry])
            if b == 0:
                continue
            for d in range(DST):
                vop(lambda e, d=d, Lb=Lb: e.tensor_tensor(
                    st3[:, d, 0:Lb], st3[:, d, 0:Lb],
                    crep.ap()[:, d * L:d * L + Lb], ALU.mult),
                    reads=[t_stt, t_crep], writes=[t_stt], gap=1)
            for half in (8, 4, 2, 1):
                v0 = st3[:, 0:half, 0:Lb]
                v1 = st3[:, half:2 * half, 0:Lb]
                vop(lambda e, v0=v0, v1=v1: e.tensor_tensor(v0, v0, v1, ALU.add),
                    reads=[t_stt], writes=[t_stt], gap=1)
            vop(lambda e, yb=yb, Lb=Lb: e.tensor_tensor(
                ycf[yb].ap()[:, 0:Lb], ycf[yb].ap()[:, 0:Lb],
                stt.ap()[:, 0:Lb], ALU.add),
                reads=[tl_ycf[yb], t_stt], writes=[tl_ycf[yb]])
            vop(lambda e, yb=yb, cc=cc, Lb=Lb: e.tensor_tensor(
                ydt.ap()[:, cc * L:(cc + 1) * L], ycf[yb].ap()[:, 0:Lb],
                gats.ap()[:, cc * L:cc * L + Lb], ALU.mult),
                reads=[tl_ycf[yb], t_gats], writes=[t_ydt])

        if b == 0:
            continue

        for sub in range(2):
            dma(xre[sub % 2].ap(), d_x[r0 + sub * 128: r0 + (sub + 1) * 128, :],
                reads=[tl_xre[sub % 2]], writes=[tl_xre[sub % 2]])
        for nh in range(2):
            def wslot(cc):
                return brep, t_brep, cc * 512
            for cc in range(NCC):
                wbuf, wtl, sl = wslot(cc)
                dma(wbuf.ap()[:, sl:sl + 512],
                    s_wout[cc * 128:(cc + 1) * 128, nh * 512:(nh + 1) * 512],
                    reads=[wtl], writes=[wtl])
            for cc in range(NCC):
                wbuf, wtl, sl = wslot(cc)
                for sub in range(4):
                    top(lambda e, cc=cc, sub=sub, wbuf=wbuf, sl=sl: e.matmul(
                        pw[sub].ap(),
                        ydt.ap()[:, cc * L + sub * 128:cc * L + (sub + 1) * 128],
                        wbuf.ap()[:, sl:sl + 512], start=(cc == 0), stop=(cc == NCC - 1),
                        skip_group_check=True),
                        reads=[t_ydt, wtl], writes=[tl_pw[sub]])
            for sub in range(4):
                xb2 = sub % 2
                if nh == 0 and sub >= 2:
                    dma(xre[xb2].ap(), d_x[r0 + sub * 128: r0 + (sub + 1) * 128, :],
                        reads=[tl_xre[xb2]], writes=[tl_xre[xb2]])
                if nh == 1:
                    dma(xre[xb2].ap(), d_x[r0 + sub * 128: r0 + (sub + 1) * 128, :],
                        reads=[tl_xre[xb2]], writes=[tl_xre[xb2]])
                vop(lambda e, sub=sub, xb2=xb2, nh=nh: e.tensor_tensor(
                    outt.ap()[:, sub * H + nh * 512: sub * H + (nh + 1) * 512],
                    xre[xb2].ap()[:, nh * 512:(nh + 1) * 512],
                    pw[sub].ap(), ALU.add),
                    reads=[tl_xre[xb2], tl_pw[sub]], writes=[t_outt])

        for pair in range(2):
            subs = (2 * pair, 2 * pair + 1)
            for ici, icw in enumerate(IC_W):
                ico = IC_O[ici]
                for kc in range(8):
                    dma(brep.ap()[:, kc * L:kc * L + icw],
                        s_wgT[kc * 128:(kc + 1) * 128, ico:ico + icw],
                        reads=[t_brep], writes=[t_brep])
                    dma(crep.ap()[:, kc * L:kc * L + icw],
                        s_wuT[kc * 128:(kc + 1) * 128, ico:ico + icw],
                        reads=[t_crep], writes=[t_crep])
                for sub in subs:
                    s2 = sub % 2
                    pg_ap = pmm[0].ap() if s2 == 0 else pw[0].ap()
                    pgt = tl_pmm[0] if s2 == 0 else tl_pw[0]
                    pu_ap = pmm[1].ap() if s2 == 0 else pw[1].ap()
                    put = tl_pmm[1] if s2 == 0 else tl_pw[1]
                    for kc in range(8):
                        lt = xqT.ap()[:, kc * L + sub * 128:kc * L + (sub + 1) * 128]
                        top(lambda e, lt=lt, kc=kc, icw=icw, pg_ap=pg_ap:
                            e.matmul(pg_ap[:, 0:icw], lt,
                                     brep.ap()[:, kc * L:kc * L + icw],
                                     start=(kc == 0), stop=(kc == 7),
                                     skip_group_check=True),
                            reads=[t_xqT, t_brep], writes=[pgt])
                        top(lambda e, lt=lt, kc=kc, icw=icw, pu_ap=pu_ap:
                            e.matmul(pu_ap[:, 0:icw], lt,
                                     crep.ap()[:, kc * L:kc * L + icw],
                                     start=(kc == 0), stop=(kc == 7),
                                     skip_group_check=True),
                            reads=[t_xqT, t_crep], writes=[put])
                for sub in subs:
                    s2 = sub % 2
                    pg_ap = pmm[0].ap() if s2 == 0 else pw[0].ap()
                    pgt = tl_pmm[0] if s2 == 0 else tl_pw[0]
                    pu_ap = pmm[1].ap() if s2 == 0 else pw[1].ap()
                    put = tl_pmm[1] if s2 == 0 else tl_pw[1]
                    fb = sub % 2
                    sop(lambda e, fb=fb, icw=icw, pg_ap=pg_ap: e.activation(
                        sgb[fb].ap()[:, 0:icw], pg_ap[:, 0:icw], AF.Silu),
                        reads=[pgt], writes=[tl_sgb[fb]])
                    vop(lambda e, fb=fb, icw=icw, pu_ap=pu_ap: e.tensor_tensor(
                        hpc[fb].ap()[:, 0:icw], sgb[fb].ap()[:, 0:icw],
                        pu_ap[:, 0:icw], ALU.mult),
                        reads=[tl_sgb[fb], put], writes=[tl_hpc[fb]])
                    quant(hqp[fb].ap()[:, 0:icw], tl_hqp[fb],
                          hpc[fb].ap()[:, 0:icw], tl_hpc[fb], icw)
                    for j in range(icw // 128):
                        i22 = ico // 128 + j
                        hcol = (sub % 2) * 22 * 128 + i22 * 128
                        transpose_to(hqT.ap()[:, hcol:hcol + 128], t_hqT,
                                     hqp[fb].ap()[0:128, j * 128:(j + 1) * 128],
                                     tl_hqp[fb], 128, j % 2)
            for nh in range(2):
                def dslot(i22):
                    if i22 < 16:
                        return brep, t_brep, i22 * 512
                    return crep, t_crep, (i22 - 16) * 512
                for i22 in range(22):
                    wbuf, wtl, sl = dslot(i22)
                    dma(wbuf.ap()[:, sl:sl + 512],
                        s_wdT[i22 * 128:(i22 + 1) * 128, nh * 512:(nh + 1) * 512],
                        reads=[wtl], writes=[wtl])
                for sub in subs:
                    vop(lambda e, sub=sub: e.memset(pw[sub % 2].ap(), 0.0),
                        reads=[], writes=[tl_pw[sub % 2]])
                for i22 in range(22):
                    wbuf, wtl, sl = dslot(i22)
                    for sub in subs:
                        hcol = (sub % 2) * 22 * 128 + i22 * 128
                        top(lambda e, hcol=hcol, sub=sub, wbuf=wbuf, sl=sl: e.matmul(
                            pw[sub % 2].ap(), hqT.ap()[:, hcol:hcol + 128],
                            wbuf.ap()[:, sl:sl + 512], start=False, stop=(i22 == 21),
                            skip_group_check=True),
                            reads=[t_hqT, wtl], writes=[tl_pw[sub % 2]])
                for sub in subs:
                    ov = outt.ap()[:, sub * H + nh * 512: sub * H + (nh + 1) * 512]
                    vop(lambda e, ov=ov, sub=sub: e.tensor_tensor(
                        ov, ov, pw[sub % 2].ap(), ALU.add),
                        reads=[t_outt, tl_pw[sub % 2]], writes=[t_outt])
        for sub in range(4):
            orow = i0 - W + sub * 128
            dma(d_out[orow:orow + 128, :], outt.ap()[:, sub * H:(sub + 1) * H],
                reads=[t_outt], writes=[t_out])

    for key in ENGS:
        for waits, body, sem, amt in S.q[key]:
            getsem(sem)
            for (ws, wv) in waits:
                getsem(ws)

    with nc.Block() as block:
        def replay(e_obj, key):
            for waits, body, sem, amt in S.q[key]:
                for (ws, wv) in waits:
                    e_obj.wait_ge(sems[ws], wv)
                inst = body(e_obj)
                inst.then_inc(sems[sem], amt)

        @block.sync
        def _(sync):
            replay(sync, "qs")
            for name, v in sorted(S.cnt.items()):
                if name.startswith("q"):
                    sync.wait_ge(sems[name], v)

        @block.vector
        def _(v):
            replay(v, "v")

        @block.scalar
        def _(s):
            replay(s, "s")

        @block.tensor
        def _(t):
            replay(t, "t")

        @block.gpsimd
        def _(g):
            replay(g, "g")

    ctx.close()
    return nc


_CACHE = {}


def kernel(L0_attn_x, L0_attn_W_in, L0_attn_conv_w, L0_attn_conv_b, L0_attn_W_x,
           L0_attn_W_dt, L0_attn_b_dt, L0_attn_A_log, L0_attn_D, L0_attn_W_out,
           L0_ffn_w_gate, L0_ffn_w_up, L0_ffn_w_down):
    x = np.ascontiguousarray(np.asarray(L0_attn_x, dtype=np.float32))
    if "nc" not in _CACHE:
        _CACHE["nc"] = build_nc()
    nc = _CACHE["nc"]

    f32 = lambda a: np.ascontiguousarray(np.asarray(a, dtype=np.float32))
    w_in = f32(L0_attn_W_in)
    conv_w = f32(L0_attn_conv_w).reshape(4, C)
    conv_b = f32(L0_attn_conv_b).reshape(1, C)
    w_x = f32(L0_attn_W_x)
    w_dt = f32(L0_attn_W_dt)
    b_dt = f32(L0_attn_b_dt).reshape(1, C)
    a_log = f32(L0_attn_A_log)
    d_vec = f32(L0_attn_D).reshape(1, C)
    w_out = f32(L0_attn_W_out)
    w_g = f32(L0_ffn_w_gate)
    w_u = f32(L0_ffn_w_up)
    w_d = f32(L0_ffn_w_down)
    ident = np.eye(128, dtype=ml_dtypes.bfloat16)

    in_maps = []
    for k in range(8):
        half = k % 2
        mask = np.full((1, W), 1.0 if half == 1 else 0.0, dtype=ml_dtypes.bfloat16)
        in_maps.append({
            "x_slab": _slab_rows_for_core(x, k),
            "w_in": w_in, "conv_w": conv_w, "conv_b": conv_b, "w_x": w_x,
            "w_dt": w_dt, "b_dt": b_dt, "a_log": a_log, "d_vec": d_vec,
            "w_out": w_out, "w_gate": w_g, "w_up": w_u, "w_down": w_d,
            "wmask": mask, "identw": ident,
        })
    res = run_bass_kernel_spmd(nc, in_maps, core_ids=list(range(8)))
    out = np.empty((4, 4096, H), np.float32)
    for k in range(8):
        seq, half = k // 2, k % 2
        out[seq, half * MAIN:(half + 1) * MAIN] = res.results[k]["out"]
    return out
